# version 50
# baseline (speedup 1.0000x reference)
import numpy as np
from contextlib import ExitStack

import concourse.bass as bass
import concourse.mybir as mybir
from concourse.bass_utils import run_bass_kernel_spmd

F32 = mybir.dt.float32
BF16 = mybir.dt.bfloat16
I32 = mybir.dt.int32
AF = mybir.ActivationFunctionType
ALU = mybir.AluOpType
AX = mybir.AxisListType

D = 4096
TOK = 16384
NCORES = 8
TPC = TOK // NCORES
G = TPC // 128
MACRO = 4
NMACRO = G // MACRO
NCH = D // 128
EPS = 1e-5
NSLOT = 8

class Sched:

    AUTO = ("vector", "scalar")

    def __init__(self):
        self.ops = {e: [] for e in ("sync", "vector", "scalar", "tensor", "gpsimd")}
        self.ctr = {}
        self.own = {}
        self.pace_pts = []

    def bind(self, eng, sem):
        self.own[eng] = sem

    def wait(self, eng, sem, val):
        self.ops[eng].append(lambda e, sem=sem, val=val: e.wait_ge(sem, val))

    def emit(self, eng, fn, sem=None, inc=0, tail_nop=False, relax=False):
        if eng in self.AUTO:
            own = self.own[eng]
            prev = self.ctr.get(own.name, 0)
            if prev > 0 and not relax:
                self.ops[eng].append(
                    lambda e, own=own, prev=prev: e.wait_ge(own, prev))
            self.ctr[own.name] = prev + 1
            if tail_nop:
                def run(e, fn=fn, own=own):
                    fn(e)
                    e.nop().then_inc(own, 1)
            else:
                def run(e, fn=fn, own=own):
                    fn(e).then_inc(own, 1)
            self.ops[eng].append(run)
        elif sem is not None:
            name = sem.name
            self.ctr[name] = self.ctr.get(name, 0) + inc
            self.ops[eng].append(
                lambda e, fn=fn, sem=sem, inc=inc: fn(e).then_inc(sem, inc))
        else:
            self.ops[eng].append(fn)

    def val(self, sem):
        return self.ctr.get(sem.name, 0)

    def checkpoint(self, sem):
        self.pace_pts.append(self.ctr.get(sem.name, 0))


def build_program(debug=False):
    nc = bass.Bass("TRN2", target_bir_lowering=False)

    x_d = nc.dram_tensor("x", [TPC, D], F32, kind="ExternalInput").ap()
    ar_d = nc.dram_tensor("ar", [128, 2208], F32, kind="ExternalInput").ap()
    br_d = nc.dram_tensor("br", [128, 2048], F32, kind="ExternalInput").ap()
    idh_d = nc.dram_tensor("idh", [128, 128], I32, kind="ExternalInput").ap()
    out_d = nc.dram_tensor("out", [TPC, D], F32, kind="ExternalOutput").ap()

    es = ExitStack()
    sb = lambda n, s, d: es.enter_context(nc.sbuf_tensor(n, s, d))
    ps = lambda n, s, d: es.enter_context(nc.psum_tensor(n, s, d))

    xslot = sb("xslot", [128, NSLOT, D], F32)
    ab = sb("ab", [128, NCH, 69], BF16)
    bb = sb("bb", [64, D], BF16)
    rsb32 = sb("rsb32", [64, 1], F32)
    rsb = sb("rsb", [64, 1], BF16)
    iden_i = sb("iden_i", [128, 128], I32)
    iden = sb("iden", [128, 128], BF16)
    XRING = MACRO + 1
    scr_a = sb("scr_a", [128, (MACRO + 1) * D], BF16)
    scr_b = sb("scr_b", [128, D], BF16)
    diagc = sb("diagc", [128, MACRO * 128], BF16)
    colrow = sb("colrow", [1, 68], BF16)
    colb = sb("colb", [128, 68], F32)
    ones128 = sb("ones128", [128, 1], BF16)
    onesrow = sb("onesrow", [1, 128], BF16)
    pre = sb("pre", [128, 64], F32)
    epsc = sb("epsc", [128, 1], F32)
    dW = sb("dW", [64, MACRO * 128], BF16)
    sq4 = sb("sq4", [128, MACRO * 4], F32)
    gmidT = sb("gmidT", [128, 128], BF16)
    gmid_er = sb("gmid_er", [64, MACRO * 128], BF16)
    SCALARS = ("sx sxx m1 m1sq v1 t1 rc v2 t2 cr ao bo ao2 v3 t3 a3 b3 r1 c2 "
               "r3 m4 m4sq sddsum v4 t4 a4 b4 bo2 tln tcp rcp cp w1 w2 w3 btot").split()
    S_ = {n: sb("s_" + n, [128, MACRO], F32) for n in SCALARS}
    bbT = sb("bbT", [128, NCH, 64], BF16)
    BBsb = sb("BBsb", [64, 64], BF16)
    Wsb = sb("Wsb", [64, MACRO * 128], BF16)
    mskd = sb("mskd", [128, 128], F32)
    iden32 = sb("iden32", [128, 128], F32)
    lg = sb("lg", [128, MACRO * 4], F32)
    lsub = sb("lsub", [128, MACRO * 4], F32)
    p4 = sb("p4", [128, MACRO * 4], F32)
    cmp6 = sb("cmp6", [128, 6 * MACRO], F32)
    rnk = sb("rnk", [128, 4 * MACRO], F32)
    rnk2 = sb("rnk2", [128, 4 * MACRO], F32)
    msk = sb("msk", [128, MACRO * 4], F32)
    zz = sb("zz", [128, MACRO * 4], F32)
    ssum = sb("ssum", [128, MACRO], F32)
    rsum = sb("rsum", [128, MACRO], F32)
    g4 = sb("g4", [128, MACRO * 4], F32)

    PSUM_SD = 280
    PSUM_SQ = PSUM_SD + MACRO
    psT = [ps("psT0", [128, 1024], BF16), ps("psT1", [128, 1024], BF16)]
    mid = [ps("mid0", [128, 512], F32), ps("mid1", [128, 512], F32)]
    dps = ps("dps", [128, 2048], F32)

    s_w = es.enter_context(nc.semaphore("s_w"))
    s_w2 = es.enter_context(nc.semaphore("s_w2"))
    s_x = [es.enter_context(nc.semaphore(f"s_x{i}")) for i in range(NSLOT)]
    s_st = [es.enter_context(nc.semaphore(f"s_st{i}")) for i in range(NSLOT)]
    s_v = es.enter_context(nc.semaphore("s_v"))
    s_s = es.enter_context(nc.semaphore("s_s"))
    s_pe = es.enter_context(nc.semaphore("s_pe"))
    s_g = es.enter_context(nc.semaphore("s_g"))

    sc = Sched()
    sc.bind("vector", s_v)
    sc.bind("scalar", s_s)
    sc.bind("gpsimd", s_g)

    arstage = scr_a[:, 8192:12608].bitcast(F32)
    brstage = scr_a[:, 12608:16704].bitcast(F32)
    sc.emit("sync", lambda e: e.dma_start(out=iden_i[:], in_=idh_d), s_g, 16)
    sc.emit("sync", lambda e: e.dma_start(out=brstage, in_=br_d), s_w, 16)
    sc.emit("sync", lambda e: e.dma_start(out=arstage, in_=ar_d), s_w2, 16)
    sc.emit("vector", lambda e: e.memset(ones128[:], 1.0))
    sc.emit("vector", lambda e: e.memset(onesrow[:], 1.0), relax=True)
    sc.emit("vector", lambda e: e.memset(epsc[:], EPS), relax=True)
    sc.wait("scalar", s_v, sc.val(s_v))
    sc.emit("scalar", lambda e: e.activation(
        S_["tln"][:, 0:1], epsc[:], AF.Ln))
    sc.wait("vector", s_g, 16)
    sc.emit("vector", lambda e: e.tensor_single_scalar(iden[:], iden_i[:], 0.0,
                                                       ALU.is_equal))
    sc.emit("vector", lambda e: e.tensor_single_scalar(iden32[:], iden_i[:], 0.0,
                                                       ALU.is_equal), relax=True)
    sc.wait("vector", s_w, 16)
    sc.emit("vector", lambda e: e.tensor_copy(
        bb[:, 0:2048], brstage[0:64, :]))
    sc.emit("vector", lambda e: e.tensor_copy(
        bb[:, 2048:4096], brstage[64:128, :]), relax=True)
    for half in range(2):
        sc.wait("tensor", s_v, sc.val(s_v))
        for c in range(16):
            last = (c == 15)
            ch = 16 * half + c
            sc.emit("tensor", lambda e, c=c, ch=ch: e.transpose(
                psT[0][0:128, 64 * c:64 * (c + 1)],
                bb[:, 128 * ch:128 * (ch + 1)], iden[0:64, 0:64]),
                s_pe if last else None, 1 if last else 0)
        sc.wait("vector", s_pe, sc.val(s_pe))
        sc.emit("vector", lambda e, half=half: e.tensor_copy(
            bbT[:, 16 * half:16 * (half + 1), :],
            psT[0][:, 0:1024].rearrange("p (c k) -> p c k", k=64)))
    sc.wait("tensor", s_v, sc.val(s_v))
    for c in range(NCH):
        last = (c == NCH - 1)
        sc.emit("tensor", lambda e, c=c: e.matmul(
            mid[0][0:64, 0:64], bbT[:, c, :], bbT[:, c, :],
            start=(c == 0), stop=(c == NCH - 1)),
            s_pe if last else None, 1 if last else 0)
    sc.wait("vector", s_pe, sc.val(s_pe))
    bb_read_done = sc.val(s_pe)
    sc.emit("vector", lambda e: e.tensor_copy(BBsb[:], mid[0][0:64, 0:64]))
    sc.wait("vector", s_w2, 16)
    sc.emit("vector", lambda e: e.tensor_copy(
        ab[:].rearrange("p a b -> p (a b)"), arstage))
    sc.wait("tensor", s_v, sc.val(s_v))
    for c in range(NCH):
        last = (c == NCH - 1)
        sc.emit("tensor", lambda e, c=c: e.matmul(
            mid[1][0:69, 0:1], ab[:, c, :], ones128[:],
            start=(c == 0), stop=(c == NCH - 1)),
            s_pe if last else None, 1 if last else 0)
    sc.wait("vector", s_pe, sc.val(s_pe))
    sc.emit("vector", lambda e: e.tensor_copy(mskd[0:68, 0:1], mid[1][0:68, 0:1]))
    sc.wait("tensor", s_v, sc.val(s_v))
    sc.emit("tensor", lambda e: e.transpose(
        mid[1][0:1, 128:196], mskd[0:68, 0:1], iden32[0:68, 0:68]), s_pe, 1)
    sc.wait("vector", s_pe, sc.val(s_pe))
    sc.emit("vector", lambda e: e.tensor_copy(colrow[:], mid[1][0:1, 128:196]))
    sc.wait("tensor", s_v, sc.val(s_v))
    sc.emit("tensor", lambda e: e.matmul(
        mid[1][:, 256:324], onesrow[:], colrow[:], start=True, stop=True),
        s_pe, 1)
    sc.wait("vector", s_pe, sc.val(s_pe))
    sc.emit("vector", lambda e: e.tensor_copy(colb[:], mid[1][:, 256:324]))
    sc.emit("vector", lambda e: e.reduce_sum(rsb32[:], bb[:], axis=AX.X))
    sc.emit("vector", lambda e: e.tensor_copy(rsb[:], rsb32[:]))

    for g in range(min(2 * MACRO, G)):
        sc.emit("sync", lambda e, g=g: e.dma_start(
            out=xslot[:, g % NSLOT, :], in_=x_d[128 * g:128 * (g + 1), :]),
            s_x[g % NSLOT], 16)

    evacT_done = []
    mm1_done = {}
    sq_done = {}
    chain_red = {}
    summm_done = {}
    dfree = {0: 0, 1: 0}
    diag_state = [0]
    diag0_state = [0]
    btot_state = [0]
    pass2_done = {}
    pass2_done_g = {}
    evacout_done = {}
    evach = {}

    def emit_pe_pacers():
        if sc.pace_pts:
            sc.wait("tensor", s_s, max(dfree.values()))
        for cp in sc.pace_pts:
            sc.wait("tensor", s_v, cp)
            sc.emit("tensor", lambda e: e.matmul(
                dps[0:1, 0:1], ones128[:], ones128[:], start=True, stop=True))
        sc.pace_pts.clear()

    def chain_rsqrt(view, src_n, dst_n, bias=None, scr="tln"):
        if bias is None:
            sc.emit("scalar", lambda e, scr=scr: e.activation(
                S_[scr][view], S_[src_n][view], AF.Ln), relax=True)
        else:
            sc.emit("scalar", lambda e, scr=scr: e.activation(
                S_[scr][view], S_[src_n][view], AF.Ln, bias=epsc[:, 0:1]),
                relax=True)
        sc.emit("scalar", lambda e, scr=scr: e.activation(
            S_[dst_n][view], S_[scr][view], AF.Exp, scale=-0.5))

    ALLV = (slice(None), slice(0, MACRO))

    def emit_chain(m):
        gbase = m * MACRO
        V = ALLV
        sc.wait("vector", s_s, sq_done[gbase + MACRO - 1])
        sq3 = sq4[:].rearrange("p (g c) -> p g c", c=4)
        sc.emit("vector", lambda e, sq3=sq3: e.reduce_sum(
            S_["sxx"][V], sq3, axis=AX.X))
        chain_red[m] = sc.val(s_v)
        sc.emit("vector", lambda e: e.tensor_tensor(
            S_["m1sq"][V], S_["m1"][V], S_["m1"][V], ALU.mult), relax=True)
        sc.emit("vector", lambda e: e.scalar_tensor_tensor(
            S_["v1"][V], S_["sxx"][V], 1.0 / D, S_["m1sq"][V],
            ALU.mult, ALU.subtract))
        sc.emit("vector", lambda e: e.tensor_scalar(
            S_["t1"][V], S_["v1"][V], EPS, None, ALU.add))
        sc.checkpoint(s_v)
        sc.wait("scalar", s_v, sc.val(s_v))
        chain_rsqrt(V, "t1", "r1", scr="tln")
        sc.emit("vector", lambda e: e.reciprocal(S_["rc"][V], S_["t1"][V]))
        sc.emit("vector", lambda e: e.tensor_tensor(
            S_["v2"][V], S_["v1"][V], S_["rc"][V], ALU.mult))
        sc.checkpoint(s_v)
        sc.wait("scalar", s_v, sc.val(s_v))
        chain_rsqrt(V, "v2", "c2", bias=EPS, scr="w2")
        sc.wait("vector", s_s, sc.val(s_s))
        sc.emit("vector", lambda e: e.tensor_tensor(
            S_["cr"][V], S_["c2"][V], S_["r1"][V], ALU.mult), relax=True)
        sc.emit("vector", lambda e: e.tensor_scalar(
            S_["ao"][V], S_["cr"][V], 1.0, None, ALU.add))
        sc.emit("vector", lambda e: e.scalar_tensor_tensor(
            S_["bo"][V], S_["cr"][V], -1.0, S_["m1"][V], ALU.mult, ALU.mult),
            relax=True)
        sc.emit("vector", lambda e: e.tensor_tensor(
            S_["ao2"][V], S_["ao"][V], S_["ao"][V], ALU.mult))
        sc.emit("vector", lambda e: e.tensor_tensor(
            S_["v3"][V], S_["ao2"][V], S_["v1"][V], ALU.mult))
        sc.checkpoint(s_v)
        sc.wait("scalar", s_v, sc.val(s_v))
        chain_rsqrt(V, "v3", "r3", bias=EPS, scr="w3")
        sc.wait("vector", s_s, sc.val(s_s))
        sc.emit("vector", lambda e: e.tensor_tensor(
            S_["a3"][V], S_["ao"][V], S_["r3"][V], ALU.mult), relax=True)

    def emit_A2_group(m, gg):
        gbase = m * MACRO
        par = m % 2
        opar = (m - 1) % 2 if m >= 1 else 1
        g = gbase + gg
        xg = xslot[:, g % NSLOT, :]
        st = lambda n, gg=gg: S_[n][:, gg:gg + 1]
        xbs = scr_a[:, (g % XRING) * D:(g % XRING + 1) * D]
        sc.wait("vector", s_x[g % NSLOT], 16 * (g // NSLOT + 1))
        if g >= XRING:
            sc.wait("vector", s_pe, pass2_done_g[g - XRING])
        cast_done = []
        for b in range(4):
            sc.emit("vector", lambda e, xg=xg, xbs=xbs, b=b: e.tensor_scalar(
                xbs[:, 1024 * b:1024 * (b + 1)], xg[:, 1024 * b:1024 * (b + 1)],
                1.0, None, ALU.mult), relax=True)
            cast_done.append(sc.val(s_v))
        for b in range(4):
            parT = psT[b % 2]
            gb = g * 4 + b
            sc.wait("tensor", s_v, cast_done[b])
            if gb >= 2:
                sc.wait("tensor", s_v, evacT_done[gb - 2])
            for c in range(8):
                k = 8 * b + c
                last = (c == 7)
                sc.emit("tensor", lambda e, parT=parT, c=c, k=k, xbs=xbs:
                        e.transpose(parT[:, 128 * c:128 * (c + 1)],
                                    xbs[:, 128 * k:128 * (k + 1)], iden[:]),
                        s_pe if last else None, 1 if last else 0)
            tb_done = sc.val(s_pe)
            sc.wait("vector", s_pe, tb_done)
            if b == 0 and g > 0:
                sc.wait("vector", s_pe, mm1_done[g - 1])
            sc.emit("vector", lambda e, parT=parT, b=b: e.tensor_copy(
                scr_b[:, 1024 * b:1024 * (b + 1)], parT[:]), relax=True)
            evacT_done.append(sc.val(s_v))

        sc.wait("tensor", s_v, evacT_done[-1])
        for c in range(NCH):
            last = (c == NCH - 1)
            sc.emit("tensor", lambda e, c=c, gg=gg, par=par: e.matmul(
                mid[par][:, 69 * gg:69 * (gg + 1)],
                scr_b[:, 128 * c:128 * (c + 1)],
                ab[:, c, :],
                start=(c == 0), stop=last, skip_group_check=True),
                s_pe if last else None, 1 if last else 0)
        mm1_done[g] = sc.val(s_pe)
        sc.wait("vector", s_pe, mm1_done[g])
        sc.emit("vector", lambda e, st=st, gg=gg, par=par: e.tensor_scalar(
            st("m1"), mid[par][:, 69 * gg + 68:69 * gg + 69], 1.0 / D,
            None, ALU.mult), relax=True)
        sc.emit("vector", lambda e, st=st, gg=gg, par=par: e.tensor_scalar(
            st("w1"), mid[par][:, 69 * gg + 68:69 * gg + 69], -1.0 / D,
            None, ALU.mult), relax=True)
        sc.emit("vector", lambda e, st=st, gg=gg, par=par: e.scalar_tensor_tensor(
            lg[:, 4 * gg:4 * (gg + 1)], colb[:, 64:68], st("w1"),
            mid[par][:, 69 * gg + 64:69 * gg + 68], ALU.mult, ALU.add))

    bbTf = bbT[:].rearrange("p a b -> p (a b)")

    def emit_square(m, gg):
        g = m * MACRO + gg
        sc.wait("scalar", s_x[g % NSLOT], 16 * (g // NSLOT + 1))
        if m == 0 and gg == 0:
            sc.wait("scalar", s_pe, bb_read_done)
        if m >= 1:
            sc.wait("scalar", s_v, chain_red[m - 1])
        for c in range(4):
            sc.emit("scalar", lambda e, g=g, c=c, gg=gg: e.activation(
                bbTf[:, 1024 * (c % 2):1024 * (c % 2 + 1)],
                xslot[:, g % NSLOT, 1024 * c:1024 * (c + 1)],
                AF.Square, accum_out=sq4[:, 4 * gg + c:4 * gg + c + 1]),
                relax=(c == 1))
        sq_done[g] = sc.val(s_s)

    def emit_B(m):
        par = m % 2
        lg3 = lg[:].rearrange("p (g e) -> p g e", e=4)
        sc.emit("vector", lambda e: e.reduce_max(ssum[:], lg3, axis=AX.X),
                relax=True)
        sc.emit("vector", lambda e, lg3=lg3: e.tensor_tensor(
            pre[:, 0:16].rearrange("p (g e) -> p g e", e=4), lg3,
            ssum[:].unsqueeze(2).broadcast_to([128, MACRO, 4]), ALU.subtract))
        sc.emit("vector", lambda e: e.tensor_tensor(
            lsub[:].rearrange("p (g e) -> p g e", e=4),
            pre[:, 0:16].rearrange("p (g e) -> p g e", e=4),
            S_["a3"][:, 0:MACRO].unsqueeze(2).broadcast_to([128, MACRO, 4]),
            ALU.mult))
        sc.checkpoint(s_v)
        sc.wait("scalar", s_v, sc.val(s_v))
        sc.emit("scalar", lambda e: e.activation(p4[:], lsub[:], AF.Exp),
                relax=True)
        pairs = [(0, 1), (0, 2), (0, 3), (1, 2), (1, 3), (2, 3)]
        for k, (i, j) in enumerate(pairs):
            sc.emit("vector", lambda e, k=k, i=i, j=j: e.tensor_tensor(
                cmp6[:, MACRO * k:MACRO * (k + 1)], lg3[:, :, i:i + 1],
                lg3[:, :, j:j + 1], ALU.is_ge), relax=True)
        cc = lambda i, j: cmp6[:, MACRO * pairs.index((i, j)):
                               MACRO * (pairs.index((i, j)) + 1)]
        m3v = msk[:].rearrange("p (g e) -> p g e", e=4)
        rk = lambda k: rnk[:, MACRO * k:MACRO * (k + 1)]
        rk2 = lambda k: rnk2[:, MACRO * k:MACRO * (k + 1)]
        sc.emit("vector", lambda e: e.tensor_tensor(rk(0), cc(0, 1), cc(0, 2), ALU.add))
        sc.emit("vector", lambda e: e.tensor_tensor(rk(1), cc(1, 2), cc(1, 3), ALU.add), relax=True)
        sc.emit("vector", lambda e: e.tensor_tensor(rk(2), cc(0, 2), cc(1, 2), ALU.add), relax=True)
        sc.emit("vector", lambda e: e.tensor_tensor(rk(3), cc(0, 3), cc(1, 3), ALU.add), relax=True)
        sc.emit("vector", lambda e: e.tensor_tensor(rk2(0), rk(0), cc(0, 3), ALU.add))
        sc.emit("vector", lambda e: e.tensor_tensor(rk2(1), rk(1), cc(0, 1), ALU.subtract), relax=True)
        sc.emit("vector", lambda e: e.tensor_tensor(rk2(2), cc(2, 3), rk(2), ALU.subtract), relax=True)
        sc.emit("vector", lambda e: e.tensor_tensor(rk2(3), rk(3), cc(2, 3), ALU.add), relax=True)
        sc.emit("vector", lambda e: e.tensor_single_scalar(
            m3v[:, :, 0:1], rk2(0), 2.0, ALU.is_ge))
        sc.emit("vector", lambda e: e.tensor_single_scalar(
            m3v[:, :, 1:2], rk2(1), 1.0, ALU.is_ge), relax=True)
        sc.emit("vector", lambda e: e.tensor_single_scalar(
            m3v[:, :, 2:3], rk2(2), 0.0, ALU.is_ge), relax=True)
        sc.emit("vector", lambda e: e.tensor_single_scalar(
            m3v[:, :, 3:4], rk2(3), 1.0, ALU.is_le), relax=True)
        sc.checkpoint(s_v)
        sc.wait("vector", s_s, sc.val(s_s))
        sc.emit("vector", lambda e: e.tensor_tensor(zz[:], msk[:], p4[:], ALU.mult))
        z3 = zz[:].rearrange("p (g e) -> p g e", e=4)
        sc.emit("vector", lambda e: e.reduce_sum(ssum[:], z3, axis=AX.X))
        sc.emit("vector", lambda e: e.reciprocal(rsum[:], ssum[:]))
        sc.emit("vector", lambda e, z3=z3: e.scalar_tensor_tensor(
            g4[:].rearrange("p (g e) -> p g e", e=4), z3, 4.0,
            rsum[:].unsqueeze(2).broadcast_to([128, MACRO, 4]),
            ALU.mult, ALU.mult))

    def emit_C1(m):
        gbase = m * MACRO
        par = m % 2
        emit_pe_pacers()
        gmid_ready = {}
        for pr in range(MACRO // 2):
            for gg in (2 * pr, 2 * pr + 1):
                if m >= 1:
                    sc.wait("vector", s_pe, pass2_done_g[(m - 1) * MACRO + gg])
                half = 64 * (gg & 1)
                ppb = pre[:] if (gg & 1) == 0 else mskd[:, 0:64]
                sc.emit("vector", lambda e, gg=gg, par=par, ppb=ppb:
                        e.scalar_tensor_tensor(
                    ppb, colb[:, 0:64], S_["w1"][:, gg:gg + 1],
                    mid[par][:, 69 * gg:69 * gg + 64], ALU.mult, ALU.add),
                    relax=True)
                sc.emit("vector", lambda e, gg=gg, half=half, ppb=ppb:
                        e.tensor_tensor(
                    gmidT[:, half:half + 64].rearrange("p (a b) -> p a b", b=16),
                    ppb.rearrange("p (a b) -> p a b", b=16),
                    g4[:, 4 * gg:4 * (gg + 1)].unsqueeze(2).broadcast_to(
                        [128, 4, 16]), ALU.mult))
            gmidT_ready = sc.val(s_v)
            sc.wait("tensor", s_v, gmidT_ready)
            sc.emit("tensor", lambda e, pr=pr: e.transpose(
                psT[pr % 2][0:128, 0:128], gmidT[:], iden[:]), s_pe, 1)
            sc.wait("vector", s_pe, sc.val(s_pe))
            sc.emit("vector", lambda e, pr=pr: e.tensor_copy(
                gmid_er[:, 256 * pr:256 * pr + 128],
                psT[pr % 2][0:64, 0:128]), relax=True)
            sc.emit("vector", lambda e, pr=pr: e.tensor_copy(
                gmid_er[:, 256 * pr + 128:256 * pr + 256],
                psT[pr % 2][64:128, 0:128]), relax=True)
            gmid_ready[2 * pr] = gmid_ready[2 * pr + 1] = sc.val(s_v)
            sc.wait("tensor", s_v, gmid_ready[2 * pr])
            for gg in (2 * pr, 2 * pr + 1):
                last = (gg == MACRO - 1)
                sc.emit("tensor", lambda e, gg=gg, opar=1 - par: e.matmul(
                    mid[opar][0:64, 128 * gg:128 * (gg + 1)], BBsb[:],
                    gmid_er[:, 128 * gg:128 * (gg + 1)], start=True, stop=True),
                    s_pe if last else None, 1 if last else 0)
        opar = 1 - par
        sc.wait("vector", s_pe, sc.val(s_pe))
        sc.emit("vector", lambda e: e.tensor_copy(
            Wsb[:], mid[opar][0:64, 0:128 * MACRO]), relax=True)
        sc.emit("vector", lambda e: e.tensor_tensor(
            dW[:], gmid_er[:], Wsb[:], ALU.mult))
        sc.wait("tensor", s_v, sc.val(s_v))
        for gg in range(MACRO):
            g = gbase + gg
            sc.emit("tensor", lambda e, gg=gg: e.matmul(
                mid[par][:, PSUM_SQ + gg:PSUM_SQ + gg + 1],
                dW[:, 128 * gg:128 * (gg + 1)], ones128[0:64, :],
                start=True, stop=True))
            sc.emit("tensor", lambda e, gg=gg: e.matmul(
                mid[par][:, PSUM_SD + gg:PSUM_SD + gg + 1],
                gmid_er[:, 128 * gg:128 * (gg + 1)], rsb[:],
                start=True, stop=True), s_pe, 1)
            summm_done[g] = sc.val(s_pe)

    def emit_C2a(m):
        gbase = m * MACRO
        par = m % 2
        V = ALLV
        sc.wait("vector", s_pe, summm_done[gbase + MACRO - 1])
        sc.emit("vector", lambda e, par=par: e.tensor_scalar(
            S_["m4"][V], mid[par][:, PSUM_SD:PSUM_SD + MACRO], 1.0 / D,
            None, ALU.mult), relax=True)
        sc.emit("vector", lambda e: e.tensor_tensor(
            S_["m4sq"][V], S_["m4"][V], S_["m4"][V], ALU.mult))
        sc.emit("vector", lambda e, par=par: e.scalar_tensor_tensor(
            S_["v4"][V], mid[par][:, PSUM_SQ:PSUM_SQ + MACRO], 1.0 / D,
            S_["m4sq"][V], ALU.mult, ALU.subtract))
        sc.checkpoint(s_v)
        sc.wait("scalar", s_v, sc.val(s_v))
        chain_rsqrt(V, "v4", "a4", bias=EPS, scr="t2")
        sc.wait("vector", s_s, sc.val(s_s))
        sc.emit("vector", lambda e: e.scalar_tensor_tensor(
            S_["b4"][V], S_["a4"][V], -1.0, S_["m4"][V], ALU.mult, ALU.mult),
            relax=True)
        sc.emit("vector", lambda e: e.tensor_tensor(
            S_["btot"][V], S_["bo"][V], S_["b4"][V], ALU.add))
        btot_state[0] = sc.val(s_v)
        sc.emit("vector", lambda e: e.reciprocal(S_["rcp"][V], S_["a4"][V]),
                relax=True)
        sc.emit("vector", lambda e: e.tensor_tensor(
            S_["cp"][V], S_["ao"][V], S_["rcp"][V], ALU.mult))
        sc.emit("vector", lambda e: e.tensor_tensor(
            diagc[:, 0:128], iden[:],
            S_["cp"][:, 0:1].broadcast_to([128, 128]), ALU.mult))
        diag0_state[0] = sc.val(s_v)
        sc.emit("vector", lambda e: e.tensor_tensor(
            diagc[:, 128:128 * MACRO].rearrange("p (g c) -> p g c", c=128),
            iden[:].unsqueeze(1).broadcast_to([128, MACRO - 1, 128]),
            S_["cp"][:, 1:MACRO].unsqueeze(2).broadcast_to(
                [128, MACRO - 1, 128]), ALU.mult), relax=True)
        diag_state[0] = sc.val(s_v)

    def emit_C2b_A2(m):
        gbase = m * MACRO
        par = m % 2
        emit_pe_pacers()
        btot_ready = btot_state[0]
        for gg in range(MACRO):
            g = gbase + gg
            lhs = gmid_er[:, 128 * gg:128 * (gg + 1)]
            dgc = diagc[:, 128 * gg:128 * (gg + 1)]
            y3s = scr_a[:, (g % XRING) * D:(g % XRING + 1) * D]
            sc.wait("tensor", s_v, diag0_state[0] if gg == 0 else diag_state[0])
            for h in range(4):
                dh = dps[:, 1024 * (h % 2):1024 * (h % 2 + 1)]
                sc.wait("tensor", s_s, dfree[h % 2])
                for q in range(2):
                    col = 1024 * h + 512 * q
                    sc.emit("tensor", lambda e, q=q, col=col, lhs=lhs, dh=dh:
                            e.matmul(dh[:, 512 * q:512 * (q + 1)], lhs,
                                     bb[:, col:col + 512], start=True, stop=False))
                for q in range(2):
                    col = 1024 * h + 512 * q
                    sc.emit("tensor", lambda e, q=q, col=col, dgc=dgc, dh=dh,
                            y3s=y3s: e.matmul(
                                dh[:, 512 * q:512 * (q + 1)], dgc,
                                y3s[:, col:col + 512], start=False, stop=True),
                            s_pe if q == 1 else None, 1 if q == 1 else 0)
                sc.wait("scalar", s_pe, sc.val(s_pe))
                sc.wait("scalar", s_v, btot_ready)
                sc.emit("scalar", lambda e, g=g, h=h, gg=gg, dh=dh: e.activation(
                    xslot[:, g % NSLOT, 1024 * h:1024 * (h + 1)], dh,
                    AF.Identity, bias=S_["btot"][:, gg:gg + 1],
                    scale=S_["a4"][:, gg:gg + 1]), relax=True)
                dfree[h % 2] = sc.val(s_s)
                evach[(g, h)] = sc.val(s_s)
            pass2_done[m] = pass2_done_g[g] = sc.val(s_pe)
            evacout_done[g] = sc.val(s_s)
            if m + 1 < NMACRO:
                emit_A2_group(m + 1, gg)
                emit_square(m + 1, gg)

    def emit_stores(m):
        for gg in range(MACRO):
            g = m * MACRO + gg
            for hh in range(2):
                sc.wait("sync", s_s, evach[(g, 2 * hh + 1)])
                sc.emit("sync", lambda e, g=g, hh=hh: e.dma_start(
                    out=out_d[128 * g:128 * (g + 1), 2048 * hh:2048 * (hh + 1)],
                    in_=xslot[:, g % NSLOT, 2048 * hh:2048 * (hh + 1)]),
                    s_st[g % NSLOT], 16)

    def emit_loads(m):
        if m >= NMACRO:
            return
        for gg in range(MACRO):
            g = m * MACRO + gg
            if g >= NSLOT:
                sc.wait("sync", s_st[g % NSLOT], 32 * ((g - NSLOT) // NSLOT + 1))
            sc.emit("sync", lambda e, g=g: e.dma_start(
                out=xslot[:, g % NSLOT, :],
                in_=x_d[128 * g:128 * (g + 1), :]), s_x[g % NSLOT], 16)

    for gg in range(MACRO):
        emit_A2_group(0, gg)
        emit_square(0, gg)
    emit_chain(0)
    emit_B(0)
    for m in range(NMACRO):
        emit_C1(m)
        emit_C2a(m)
        emit_C2b_A2(m)
        if m + 1 < NMACRO:
            emit_chain(m + 1)
            emit_B(m + 1)
        emit_stores(m)
        emit_loads(m + 2)

    for s in range(NSLOT):
        if sc.val(s_st[s]):
            sc.wait("sync", s_st[s], sc.val(s_st[s]))

    with es:
        blk = es.enter_context(nc.Block())

        @blk.sync
        def _(e):
            for f in sc.ops["sync"]:
                f(e)

        @blk.vector
        def _(e):
            for f in sc.ops["vector"]:
                f(e)

        @blk.scalar
        def _(e):
            for f in sc.ops["scalar"]:
                f(e)

        @blk.tensor
        def _(e):
            for f in sc.ops["tensor"]:
                f(e)

        @blk.gpsimd
        def _(e):
            for f in sc.ops["gpsimd"]:
                f(e)

    return nc


_NC_CACHE = []


def kernel(L0_attn_x, L0_ffn_router_w, L0_ffn_A, L0_ffn_B):
    x = np.ascontiguousarray(np.asarray(L0_attn_x, dtype=np.float32).reshape(TOK, D))
    rw = np.asarray(L0_ffn_router_w, dtype=np.float32)
    A = np.asarray(L0_ffn_A, dtype=np.float32)
    B = np.asarray(L0_ffn_B, dtype=np.float32)
    ar = np.ascontiguousarray(
        np.concatenate([A.reshape(64, D).T, rw.T,
                        np.ones((D, 1), np.float32)], axis=1)
        .reshape(32, 128, 69).transpose(1, 0, 2).reshape(128, 2208))
    br = np.ascontiguousarray(
        B.transpose(0, 2, 1).reshape(64, 2, 2048)
        .transpose(1, 0, 2).reshape(128, 2048))
    idh = np.ascontiguousarray(
        (np.arange(128)[None, :] - np.arange(128)[:, None]).astype(np.int32))

    if not _NC_CACHE:
        _NC_CACHE.append(build_program())
    nc = _NC_CACHE[0]
    in_maps = [
        {"x": np.ascontiguousarray(x[i * TPC:(i + 1) * TPC]), "ar": ar,
         "br": br, "idh": idh}
        for i in range(NCORES)
    ]
    res = run_bass_kernel_spmd(nc, in_maps, core_ids=list(range(NCORES)))
    out = np.concatenate([r["out"] for r in res.results], axis=0)
    return out.reshape(1, TOK, D)


# version 51
# speedup vs baseline: 1.0189x; 1.0189x over previous
import numpy as np
from contextlib import ExitStack

import concourse.bass as bass
import concourse.mybir as mybir
from concourse.bass_utils import run_bass_kernel_spmd

F32 = mybir.dt.float32
BF16 = mybir.dt.bfloat16
I32 = mybir.dt.int32
AF = mybir.ActivationFunctionType
ALU = mybir.AluOpType
AX = mybir.AxisListType

D = 4096
TOK = 16384
NCORES = 8
TPC = TOK // NCORES
G = TPC // 128
MACRO = 4
NMACRO = G // MACRO
NCH = D // 128
EPS = 1e-5
NSLOT = 8

class Sched:

    AUTO = ("vector", "scalar")

    def __init__(self):
        self.ops = {e: [] for e in ("sync", "vector", "scalar", "tensor", "gpsimd")}
        self.ctr = {}
        self.own = {}
        self.pace_pts = []

    def bind(self, eng, sem):
        self.own[eng] = sem

    def wait(self, eng, sem, val):
        self.ops[eng].append(lambda e, sem=sem, val=val: e.wait_ge(sem, val))

    def emit(self, eng, fn, sem=None, inc=0, tail_nop=False, relax=False):
        if eng in self.AUTO:
            own = self.own[eng]
            prev = self.ctr.get(own.name, 0)
            if prev > 0 and not relax:
                self.ops[eng].append(
                    lambda e, own=own, prev=prev: e.wait_ge(own, prev))
            self.ctr[own.name] = prev + 1
            if tail_nop:
                def run(e, fn=fn, own=own):
                    fn(e)
                    e.nop().then_inc(own, 1)
            else:
                def run(e, fn=fn, own=own):
                    fn(e).then_inc(own, 1)
            self.ops[eng].append(run)
        elif sem is not None:
            name = sem.name
            self.ctr[name] = self.ctr.get(name, 0) + inc
            self.ops[eng].append(
                lambda e, fn=fn, sem=sem, inc=inc: fn(e).then_inc(sem, inc))
        else:
            self.ops[eng].append(fn)

    def val(self, sem):
        return self.ctr.get(sem.name, 0)

    def checkpoint(self, sem):
        self.pace_pts.append(self.ctr.get(sem.name, 0))


def build_program(debug=False):
    nc = bass.Bass("TRN2", target_bir_lowering=False)

    x_d = nc.dram_tensor("x", [TPC, D], F32, kind="ExternalInput").ap()
    ar_d = nc.dram_tensor("ar", [128, 2208], F32, kind="ExternalInput").ap()
    br_d = nc.dram_tensor("br", [128, 2048], F32, kind="ExternalInput").ap()
    idh_d = nc.dram_tensor("idh", [128, 128], I32, kind="ExternalInput").ap()
    out_d = nc.dram_tensor("out", [TPC, D], F32, kind="ExternalOutput").ap()

    es = ExitStack()
    sb = lambda n, s, d: es.enter_context(nc.sbuf_tensor(n, s, d))
    ps = lambda n, s, d: es.enter_context(nc.psum_tensor(n, s, d))

    xslot = sb("xslot", [128, NSLOT, D], F32)
    ab = sb("ab", [128, NCH, 69], BF16)
    bb = sb("bb", [64, D], BF16)
    rsb32 = sb("rsb32", [64, 1], F32)
    rsb = sb("rsb", [64, 1], BF16)
    iden_i = sb("iden_i", [128, 128], I32)
    iden = sb("iden", [128, 128], BF16)
    XRING = MACRO + 1
    scr_a = sb("scr_a", [128, (MACRO + 1) * D], BF16)
    scr_b = sb("scr_b", [128, D], BF16)
    diagc = sb("diagc", [128, MACRO * 128], BF16)
    colrow = sb("colrow", [1, 68], BF16)
    colb = sb("colb", [128, 68], F32)
    ones128 = sb("ones128", [128, 1], BF16)
    onesrow = sb("onesrow", [1, 128], BF16)
    pre = sb("pre", [128, 64], F32)
    epsc = sb("epsc", [128, 1], F32)
    dW = sb("dW", [64, MACRO * 128], BF16)
    sq4 = sb("sq4", [128, MACRO * 4], F32)
    gmidT = sb("gmidT", [128, 128], BF16)
    gmid_er = sb("gmid_er", [64, MACRO * 128], BF16)
    SCALARS = ("sx sxx m1 m1sq v1 t1 rc v2 t2 cr ao bo ao2 v3 t3 a3 b3 r1 c2 "
               "r3 m4 m4sq sddsum v4 t4 a4 b4 bo2 tln tcp rcp cp w1 w2 w3 btot").split()
    S_ = {n: sb("s_" + n, [128, MACRO], F32) for n in SCALARS}
    bbT = sb("bbT", [128, NCH, 64], BF16)
    BBsb = sb("BBsb", [64, 64], BF16)
    Wsb = sb("Wsb", [64, MACRO * 128], BF16)
    mskd = sb("mskd", [128, 128], F32)
    iden32 = sb("iden32", [128, 128], F32)
    lg = sb("lg", [128, MACRO * 4], F32)
    lsub = sb("lsub", [128, MACRO * 4], F32)
    p4 = sb("p4", [128, MACRO * 4], F32)
    cmp6 = sb("cmp6", [128, 6 * MACRO], F32)
    rnk = sb("rnk", [128, 4 * MACRO], F32)
    rnk2 = sb("rnk2", [128, 4 * MACRO], F32)
    msk = sb("msk", [128, MACRO * 4], F32)
    zz = sb("zz", [128, MACRO * 4], F32)
    ssum = sb("ssum", [128, MACRO], F32)
    rsum = sb("rsum", [128, MACRO], F32)
    g4 = sb("g4", [128, MACRO * 4], F32)

    PSUM_SD = 280
    PSUM_SQ = PSUM_SD + MACRO
    psT = [ps("psT0", [128, 1024], BF16), ps("psT1", [128, 1024], BF16)]
    mid = [ps("mid0", [128, 512], F32), ps("mid1", [128, 512], F32)]
    dps = ps("dps", [128, 2048], F32)

    s_w = es.enter_context(nc.semaphore("s_w"))
    s_w2 = es.enter_context(nc.semaphore("s_w2"))
    s_x = [es.enter_context(nc.semaphore(f"s_x{i}")) for i in range(NSLOT)]
    s_st = [es.enter_context(nc.semaphore(f"s_st{i}")) for i in range(NSLOT)]
    s_v = es.enter_context(nc.semaphore("s_v"))
    s_s = es.enter_context(nc.semaphore("s_s"))
    s_pe = es.enter_context(nc.semaphore("s_pe"))
    s_g = es.enter_context(nc.semaphore("s_g"))

    sc = Sched()
    sc.bind("vector", s_v)
    sc.bind("scalar", s_s)
    sc.bind("gpsimd", s_g)

    arstage = scr_a[:, 8192:12608].bitcast(F32)
    brstage = scr_a[:, 12608:16704].bitcast(F32)
    sc.emit("sync", lambda e: e.dma_start(out=iden_i[:], in_=idh_d), s_g, 16)
    sc.emit("sync", lambda e: e.dma_start(out=brstage, in_=br_d), s_w, 16)
    sc.emit("sync", lambda e: e.dma_start(out=arstage, in_=ar_d), s_w2, 16)
    sc.emit("vector", lambda e: e.memset(ones128[:], 1.0))
    sc.emit("vector", lambda e: e.memset(onesrow[:], 1.0), relax=True)
    sc.emit("vector", lambda e: e.memset(epsc[:], EPS), relax=True)
    sc.wait("scalar", s_v, sc.val(s_v))
    sc.emit("scalar", lambda e: e.activation(
        S_["tln"][:, 0:1], epsc[:], AF.Ln))
    sc.wait("vector", s_g, 16)
    sc.emit("vector", lambda e: e.tensor_single_scalar(iden[:], iden_i[:], 0.0,
                                                       ALU.is_equal))
    sc.emit("vector", lambda e: e.tensor_single_scalar(iden32[:], iden_i[:], 0.0,
                                                       ALU.is_equal), relax=True)
    sc.wait("vector", s_w, 16)
    sc.emit("vector", lambda e: e.tensor_copy(
        bb[:, 0:2048], brstage[0:64, :]))
    sc.emit("vector", lambda e: e.tensor_copy(
        bb[:, 2048:4096], brstage[64:128, :]), relax=True)
    for half in range(2):
        sc.wait("tensor", s_v, sc.val(s_v))
        for c in range(16):
            last = (c == 15)
            ch = 16 * half + c
            sc.emit("tensor", lambda e, c=c, ch=ch: e.transpose(
                psT[0][0:128, 64 * c:64 * (c + 1)],
                bb[:, 128 * ch:128 * (ch + 1)], iden[0:64, 0:64]),
                s_pe if last else None, 1 if last else 0)
        sc.wait("vector", s_pe, sc.val(s_pe))
        sc.emit("vector", lambda e, half=half: e.tensor_copy(
            bbT[:, 16 * half:16 * (half + 1), :],
            psT[0][:, 0:1024].rearrange("p (c k) -> p c k", k=64)))
    sc.wait("tensor", s_v, sc.val(s_v))
    for c in range(NCH):
        last = (c == NCH - 1)
        sc.emit("tensor", lambda e, c=c: e.matmul(
            mid[0][0:64, 0:64], bbT[:, c, :], bbT[:, c, :],
            start=(c == 0), stop=(c == NCH - 1)),
            s_pe if last else None, 1 if last else 0)
    sc.wait("vector", s_pe, sc.val(s_pe))
    bb_read_done = sc.val(s_pe)
    sc.emit("vector", lambda e: e.tensor_copy(BBsb[:], mid[0][0:64, 0:64]))
    sc.wait("vector", s_w2, 16)
    sc.emit("vector", lambda e: e.tensor_copy(
        ab[:].rearrange("p a b -> p (a b)"), arstage))
    sc.wait("tensor", s_v, sc.val(s_v))
    for c in range(NCH):
        last = (c == NCH - 1)
        sc.emit("tensor", lambda e, c=c: e.matmul(
            mid[1][0:69, 0:1], ab[:, c, :], ones128[:],
            start=(c == 0), stop=(c == NCH - 1)),
            s_pe if last else None, 1 if last else 0)
    sc.wait("vector", s_pe, sc.val(s_pe))
    sc.emit("vector", lambda e: e.tensor_copy(mskd[0:68, 0:1], mid[1][0:68, 0:1]))
    sc.wait("tensor", s_v, sc.val(s_v))
    sc.emit("tensor", lambda e: e.transpose(
        mid[1][0:1, 128:196], mskd[0:68, 0:1], iden32[0:68, 0:68]), s_pe, 1)
    sc.wait("vector", s_pe, sc.val(s_pe))
    sc.emit("vector", lambda e: e.tensor_copy(colrow[:], mid[1][0:1, 128:196]))
    sc.wait("tensor", s_v, sc.val(s_v))
    sc.emit("tensor", lambda e: e.matmul(
        mid[1][:, 256:324], onesrow[:], colrow[:], start=True, stop=True),
        s_pe, 1)
    sc.wait("vector", s_pe, sc.val(s_pe))
    sc.emit("vector", lambda e: e.tensor_copy(colb[:], mid[1][:, 256:324]))
    sc.emit("vector", lambda e: e.reduce_sum(rsb32[:], bb[:], axis=AX.X))
    sc.emit("vector", lambda e: e.tensor_copy(rsb[:], rsb32[:]))

    for g in range(min(2 * MACRO, G)):
        sc.emit("sync", lambda e, g=g: e.dma_start(
            out=xslot[:, g % NSLOT, :], in_=x_d[128 * g:128 * (g + 1), :]),
            s_x[g % NSLOT], 16)

    evacT_done = []
    mm1_done = {}
    sq_done = {}
    chain_red = {}
    summm_done = {}
    dfree = {0: 0, 1: 0}
    diag_state = [0]
    diag0_state = [0]
    btot_state = [0]
    pass2_done = {}
    pass2_done_g = {}
    evacout_done = {}
    evach = {}

    def emit_pe_pacers():
        if sc.pace_pts:
            sc.wait("tensor", s_s, max(dfree.values()))
        for cp in sc.pace_pts:
            sc.wait("tensor", s_v, cp)
            sc.emit("tensor", lambda e: e.matmul(
                dps[0:1, 0:1], ones128[:], ones128[:], start=True, stop=True))
        sc.pace_pts.clear()

    def chain_rsqrt(view, src_n, dst_n, bias=None, scr="tln"):
        if bias is None:
            sc.emit("scalar", lambda e, scr=scr: e.activation(
                S_[scr][view], S_[src_n][view], AF.Ln), relax=True)
        else:
            sc.emit("scalar", lambda e, scr=scr: e.activation(
                S_[scr][view], S_[src_n][view], AF.Ln, bias=epsc[:, 0:1]),
                relax=True)
        sc.emit("scalar", lambda e, scr=scr: e.activation(
            S_[dst_n][view], S_[scr][view], AF.Exp, scale=-0.5))

    ALLV = (slice(None), slice(0, MACRO))

    def emit_chain(m):
        gbase = m * MACRO
        V = ALLV
        sc.wait("vector", s_s, sq_done[gbase + MACRO - 1])
        sq3 = sq4[:].rearrange("p (g c) -> p g c", c=4)
        sc.emit("vector", lambda e, sq3=sq3: e.reduce_sum(
            S_["sxx"][V], sq3, axis=AX.X))
        chain_red[m] = sc.val(s_v)
        sc.emit("vector", lambda e: e.tensor_tensor(
            S_["m1sq"][V], S_["m1"][V], S_["m1"][V], ALU.mult), relax=True)
        sc.emit("vector", lambda e: e.scalar_tensor_tensor(
            S_["v1"][V], S_["sxx"][V], 1.0 / D, S_["m1sq"][V],
            ALU.mult, ALU.subtract))
        sc.emit("vector", lambda e: e.tensor_scalar(
            S_["t1"][V], S_["v1"][V], EPS, None, ALU.add))
        sc.checkpoint(s_v)
        sc.wait("scalar", s_v, sc.val(s_v))
        chain_rsqrt(V, "t1", "r1", scr="tln")
        sc.emit("vector", lambda e: e.reciprocal(S_["rc"][V], S_["t1"][V]))
        sc.emit("vector", lambda e: e.tensor_tensor(
            S_["v2"][V], S_["v1"][V], S_["rc"][V], ALU.mult))
        sc.checkpoint(s_v)
        sc.wait("scalar", s_v, sc.val(s_v))
        chain_rsqrt(V, "v2", "c2", bias=EPS, scr="w2")
        sc.wait("vector", s_s, sc.val(s_s))
        sc.emit("vector", lambda e: e.tensor_tensor(
            S_["cr"][V], S_["c2"][V], S_["r1"][V], ALU.mult), relax=True)
        sc.emit("vector", lambda e: e.tensor_scalar(
            S_["ao"][V], S_["cr"][V], 1.0, None, ALU.add))
        sc.emit("vector", lambda e: e.scalar_tensor_tensor(
            S_["bo"][V], S_["cr"][V], -1.0, S_["m1"][V], ALU.mult, ALU.mult),
            relax=True)
        sc.emit("vector", lambda e: e.tensor_tensor(
            S_["ao2"][V], S_["ao"][V], S_["ao"][V], ALU.mult))
        sc.emit("vector", lambda e: e.tensor_tensor(
            S_["v3"][V], S_["ao2"][V], S_["v1"][V], ALU.mult))
        sc.checkpoint(s_v)
        sc.wait("scalar", s_v, sc.val(s_v))
        chain_rsqrt(V, "v3", "r3", bias=EPS, scr="w3")
        sc.wait("vector", s_s, sc.val(s_s))
        sc.emit("vector", lambda e: e.tensor_tensor(
            S_["a3"][V], S_["ao"][V], S_["r3"][V], ALU.mult), relax=True)

    def emit_A2_group(m, gg):
        gbase = m * MACRO
        par = m % 2
        opar = (m - 1) % 2 if m >= 1 else 1
        g = gbase + gg
        xg = xslot[:, g % NSLOT, :]
        st = lambda n, gg=gg: S_[n][:, gg:gg + 1]
        xbs = scr_a[:, (g % XRING) * D:(g % XRING + 1) * D]
        sc.wait("vector", s_x[g % NSLOT], 16 * (g // NSLOT + 1))
        if g >= XRING:
            sc.wait("vector", s_pe, pass2_done_g[g - XRING])
        cast_done = []
        for b in range(4):
            sc.emit("vector", lambda e, xg=xg, xbs=xbs, b=b: e.tensor_scalar(
                xbs[:, 1024 * b:1024 * (b + 1)], xg[:, 1024 * b:1024 * (b + 1)],
                1.0, None, ALU.mult), relax=True)
            cast_done.append(sc.val(s_v))
        for b in range(4):
            parT = psT[b % 2]
            gb = g * 4 + b
            sc.wait("tensor", s_v, cast_done[b])
            if gb >= 2:
                sc.wait("tensor", s_v, evacT_done[gb - 2])
            for c in range(8):
                k = 8 * b + c
                last = (c == 7)
                sc.emit("tensor", lambda e, parT=parT, c=c, k=k, xbs=xbs:
                        e.transpose(parT[:, 128 * c:128 * (c + 1)],
                                    xbs[:, 128 * k:128 * (k + 1)], iden[:]),
                        s_pe if last else None, 1 if last else 0)
            tb_done = sc.val(s_pe)
            sc.wait("vector", s_pe, tb_done)
            if b == 0 and g > 0:
                sc.wait("vector", s_pe, mm1_done[g - 1])
            sc.emit("vector", lambda e, parT=parT, b=b: e.tensor_copy(
                scr_b[:, 1024 * b:1024 * (b + 1)], parT[:]), relax=True)
            evacT_done.append(sc.val(s_v))

        sc.wait("tensor", s_v, evacT_done[-1])
        for c in range(NCH):
            last = (c == NCH - 1)
            sc.emit("tensor", lambda e, c=c, gg=gg, par=par: e.matmul(
                mid[par][:, 69 * gg:69 * (gg + 1)],
                scr_b[:, 128 * c:128 * (c + 1)],
                ab[:, c, :],
                start=(c == 0), stop=last, skip_group_check=True),
                s_pe if last else None, 1 if last else 0)
        mm1_done[g] = sc.val(s_pe)
        sc.wait("vector", s_pe, mm1_done[g])
        sc.emit("vector", lambda e, st=st, gg=gg, par=par: e.tensor_scalar(
            st("m1"), mid[par][:, 69 * gg + 68:69 * gg + 69], 1.0 / D,
            None, ALU.mult), relax=True)
        sc.emit("vector", lambda e, st=st, gg=gg, par=par: e.tensor_scalar(
            st("w1"), mid[par][:, 69 * gg + 68:69 * gg + 69], -1.0 / D,
            None, ALU.mult), relax=True)
        sc.emit("vector", lambda e, st=st, gg=gg, par=par: e.scalar_tensor_tensor(
            lg[:, 4 * gg:4 * (gg + 1)], colb[:, 64:68], st("w1"),
            mid[par][:, 69 * gg + 64:69 * gg + 68], ALU.mult, ALU.add))

    bbTf = bbT[:].rearrange("p a b -> p (a b)")

    def emit_square(m, gg):
        g = m * MACRO + gg
        sc.wait("scalar", s_x[g % NSLOT], 16 * (g // NSLOT + 1))
        if m == 0 and gg == 0:
            sc.wait("scalar", s_pe, bb_read_done)
        if m >= 1:
            sc.wait("scalar", s_v, chain_red[m - 1])
        for c in range(4):
            sc.emit("scalar", lambda e, g=g, c=c, gg=gg: e.activation(
                bbTf[:, 1024 * (c % 2):1024 * (c % 2 + 1)],
                xslot[:, g % NSLOT, 1024 * c:1024 * (c + 1)],
                AF.Square, accum_out=sq4[:, 4 * gg + c:4 * gg + c + 1]),
                relax=(c == 1 or c == 3))
        sq_done[g] = sc.val(s_s)

    def emit_B(m):
        par = m % 2
        lg3 = lg[:].rearrange("p (g e) -> p g e", e=4)
        sc.emit("vector", lambda e: e.reduce_max(ssum[:], lg3, axis=AX.X),
                relax=True)
        sc.emit("vector", lambda e, lg3=lg3: e.tensor_tensor(
            pre[:, 0:16].rearrange("p (g e) -> p g e", e=4), lg3,
            ssum[:].unsqueeze(2).broadcast_to([128, MACRO, 4]), ALU.subtract))
        sc.emit("vector", lambda e: e.tensor_tensor(
            lsub[:].rearrange("p (g e) -> p g e", e=4),
            pre[:, 0:16].rearrange("p (g e) -> p g e", e=4),
            S_["a3"][:, 0:MACRO].unsqueeze(2).broadcast_to([128, MACRO, 4]),
            ALU.mult))
        sc.checkpoint(s_v)
        sc.wait("scalar", s_v, sc.val(s_v))
        sc.emit("scalar", lambda e: e.activation(p4[:], lsub[:], AF.Exp),
                relax=True)
        pairs = [(0, 1), (0, 2), (0, 3), (1, 2), (1, 3), (2, 3)]
        for k, (i, j) in enumerate(pairs):
            sc.emit("vector", lambda e, k=k, i=i, j=j: e.tensor_tensor(
                cmp6[:, MACRO * k:MACRO * (k + 1)], lg3[:, :, i:i + 1],
                lg3[:, :, j:j + 1], ALU.is_ge), relax=True)
        cc = lambda i, j: cmp6[:, MACRO * pairs.index((i, j)):
                               MACRO * (pairs.index((i, j)) + 1)]
        m3v = msk[:].rearrange("p (g e) -> p g e", e=4)
        rk = lambda k: rnk[:, MACRO * k:MACRO * (k + 1)]
        rk2 = lambda k: rnk2[:, MACRO * k:MACRO * (k + 1)]
        sc.emit("vector", lambda e: e.tensor_tensor(rk(0), cc(0, 1), cc(0, 2), ALU.add))
        sc.emit("vector", lambda e: e.tensor_tensor(rk(1), cc(1, 2), cc(1, 3), ALU.add), relax=True)
        sc.emit("vector", lambda e: e.tensor_tensor(rk(2), cc(0, 2), cc(1, 2), ALU.add), relax=True)
        sc.emit("vector", lambda e: e.tensor_tensor(rk(3), cc(0, 3), cc(1, 3), ALU.add), relax=True)
        sc.emit("vector", lambda e: e.tensor_tensor(rk2(0), rk(0), cc(0, 3), ALU.add))
        sc.emit("vector", lambda e: e.tensor_tensor(rk2(1), rk(1), cc(0, 1), ALU.subtract), relax=True)
        sc.emit("vector", lambda e: e.tensor_tensor(rk2(2), cc(2, 3), rk(2), ALU.subtract), relax=True)
        sc.emit("vector", lambda e: e.tensor_tensor(rk2(3), rk(3), cc(2, 3), ALU.add), relax=True)
        sc.emit("vector", lambda e: e.tensor_single_scalar(
            m3v[:, :, 0:1], rk2(0), 2.0, ALU.is_ge))
        sc.emit("vector", lambda e: e.tensor_single_scalar(
            m3v[:, :, 1:2], rk2(1), 1.0, ALU.is_ge), relax=True)
        sc.emit("vector", lambda e: e.tensor_single_scalar(
            m3v[:, :, 2:3], rk2(2), 0.0, ALU.is_ge), relax=True)
        sc.emit("vector", lambda e: e.tensor_single_scalar(
            m3v[:, :, 3:4], rk2(3), 1.0, ALU.is_le), relax=True)
        sc.checkpoint(s_v)
        sc.wait("vector", s_s, sc.val(s_s))
        sc.emit("vector", lambda e: e.tensor_tensor(zz[:], msk[:], p4[:], ALU.mult))
        z3 = zz[:].rearrange("p (g e) -> p g e", e=4)
        sc.emit("vector", lambda e: e.reduce_sum(ssum[:], z3, axis=AX.X))
        sc.emit("vector", lambda e: e.reciprocal(rsum[:], ssum[:]))
        sc.emit("vector", lambda e, z3=z3: e.scalar_tensor_tensor(
            g4[:].rearrange("p (g e) -> p g e", e=4), z3, 4.0,
            rsum[:].unsqueeze(2).broadcast_to([128, MACRO, 4]),
            ALU.mult, ALU.mult))

    def emit_C1(m):
        gbase = m * MACRO
        par = m % 2
        emit_pe_pacers()
        gmid_ready = {}
        for pr in range(MACRO // 2):
            for gg in (2 * pr, 2 * pr + 1):
                if m >= 1:
                    sc.wait("vector", s_pe, pass2_done_g[(m - 1) * MACRO + gg])
                half = 64 * (gg & 1)
                ppb = pre[:] if (gg & 1) == 0 else mskd[:, 0:64]
                sc.emit("vector", lambda e, gg=gg, par=par, ppb=ppb:
                        e.scalar_tensor_tensor(
                    ppb, colb[:, 0:64], S_["w1"][:, gg:gg + 1],
                    mid[par][:, 69 * gg:69 * gg + 64], ALU.mult, ALU.add),
                    relax=True)
                sc.emit("vector", lambda e, gg=gg, half=half, ppb=ppb:
                        e.tensor_tensor(
                    gmidT[:, half:half + 64].rearrange("p (a b) -> p a b", b=16),
                    ppb.rearrange("p (a b) -> p a b", b=16),
                    g4[:, 4 * gg:4 * (gg + 1)].unsqueeze(2).broadcast_to(
                        [128, 4, 16]), ALU.mult))
            gmidT_ready = sc.val(s_v)
            sc.wait("tensor", s_v, gmidT_ready)
            sc.emit("tensor", lambda e, pr=pr: e.transpose(
                psT[pr % 2][0:128, 0:128], gmidT[:], iden[:]), s_pe, 1)
            sc.wait("vector", s_pe, sc.val(s_pe))
            sc.emit("vector", lambda e, pr=pr: e.tensor_copy(
                gmid_er[:, 256 * pr:256 * pr + 128],
                psT[pr % 2][0:64, 0:128]), relax=True)
            sc.emit("vector", lambda e, pr=pr: e.tensor_copy(
                gmid_er[:, 256 * pr + 128:256 * pr + 256],
                psT[pr % 2][64:128, 0:128]), relax=True)
            gmid_ready[2 * pr] = gmid_ready[2 * pr + 1] = sc.val(s_v)
            sc.wait("tensor", s_v, gmid_ready[2 * pr])
            for gg in (2 * pr, 2 * pr + 1):
                last = (gg == MACRO - 1)
                sc.emit("tensor", lambda e, gg=gg, opar=1 - par: e.matmul(
                    mid[opar][0:64, 128 * gg:128 * (gg + 1)], BBsb[:],
                    gmid_er[:, 128 * gg:128 * (gg + 1)], start=True, stop=True),
                    s_pe if last else None, 1 if last else 0)
        opar = 1 - par
        sc.wait("vector", s_pe, sc.val(s_pe))
        sc.emit("vector", lambda e: e.tensor_copy(
            Wsb[:], mid[opar][0:64, 0:128 * MACRO]), relax=True)
        sc.emit("vector", lambda e: e.tensor_tensor(
            dW[:], gmid_er[:], Wsb[:], ALU.mult))
        sc.wait("tensor", s_v, sc.val(s_v))
        for gg in range(MACRO):
            g = gbase + gg
            sc.emit("tensor", lambda e, gg=gg: e.matmul(
                mid[par][:, PSUM_SQ + gg:PSUM_SQ + gg + 1],
                dW[:, 128 * gg:128 * (gg + 1)], ones128[0:64, :],
                start=True, stop=True))
            sc.emit("tensor", lambda e, gg=gg: e.matmul(
                mid[par][:, PSUM_SD + gg:PSUM_SD + gg + 1],
                gmid_er[:, 128 * gg:128 * (gg + 1)], rsb[:],
                start=True, stop=True), s_pe, 1)
            summm_done[g] = sc.val(s_pe)

    def emit_C2a(m):
        gbase = m * MACRO
        par = m % 2
        V = ALLV
        sc.wait("vector", s_pe, summm_done[gbase + MACRO - 1])
        sc.emit("vector", lambda e, par=par: e.tensor_scalar(
            S_["m4"][V], mid[par][:, PSUM_SD:PSUM_SD + MACRO], 1.0 / D,
            None, ALU.mult), relax=True)
        sc.emit("vector", lambda e: e.tensor_tensor(
            S_["m4sq"][V], S_["m4"][V], S_["m4"][V], ALU.mult))
        sc.emit("vector", lambda e, par=par: e.scalar_tensor_tensor(
            S_["v4"][V], mid[par][:, PSUM_SQ:PSUM_SQ + MACRO], 1.0 / D,
            S_["m4sq"][V], ALU.mult, ALU.subtract))
        sc.checkpoint(s_v)
        sc.wait("scalar", s_v, sc.val(s_v))
        chain_rsqrt(V, "v4", "a4", bias=EPS, scr="t2")
        sc.wait("vector", s_s, sc.val(s_s))
        sc.emit("vector", lambda e: e.scalar_tensor_tensor(
            S_["b4"][V], S_["a4"][V], -1.0, S_["m4"][V], ALU.mult, ALU.mult),
            relax=True)
        sc.emit("vector", lambda e: e.tensor_tensor(
            S_["btot"][V], S_["bo"][V], S_["b4"][V], ALU.add))
        btot_state[0] = sc.val(s_v)
        sc.emit("vector", lambda e: e.reciprocal(S_["rcp"][V], S_["a4"][V]),
                relax=True)
        sc.emit("vector", lambda e: e.tensor_tensor(
            S_["cp"][V], S_["ao"][V], S_["rcp"][V], ALU.mult))
        sc.emit("vector", lambda e: e.tensor_tensor(
            diagc[:, 0:128], iden[:],
            S_["cp"][:, 0:1].broadcast_to([128, 128]), ALU.mult))
        diag0_state[0] = sc.val(s_v)
        sc.emit("vector", lambda e: e.tensor_tensor(
            diagc[:, 128:128 * MACRO].rearrange("p (g c) -> p g c", c=128),
            iden[:].unsqueeze(1).broadcast_to([128, MACRO - 1, 128]),
            S_["cp"][:, 1:MACRO].unsqueeze(2).broadcast_to(
                [128, MACRO - 1, 128]), ALU.mult), relax=True)
        diag_state[0] = sc.val(s_v)

    def emit_C2b_A2(m):
        gbase = m * MACRO
        par = m % 2
        emit_pe_pacers()
        btot_ready = btot_state[0]
        for gg in range(MACRO):
            g = gbase + gg
            lhs = gmid_er[:, 128 * gg:128 * (gg + 1)]
            dgc = diagc[:, 128 * gg:128 * (gg + 1)]
            y3s = scr_a[:, (g % XRING) * D:(g % XRING + 1) * D]
            sc.wait("tensor", s_v, diag0_state[0] if gg == 0 else diag_state[0])
            for h in range(4):
                dh = dps[:, 1024 * (h % 2):1024 * (h % 2 + 1)]
                sc.wait("tensor", s_s, dfree[h % 2])
                for q in range(2):
                    col = 1024 * h + 512 * q
                    sc.emit("tensor", lambda e, q=q, col=col, lhs=lhs, dh=dh:
                            e.matmul(dh[:, 512 * q:512 * (q + 1)], lhs,
                                     bb[:, col:col + 512], start=True, stop=False))
                for q in range(2):
                    col = 1024 * h + 512 * q
                    sc.emit("tensor", lambda e, q=q, col=col, dgc=dgc, dh=dh,
                            y3s=y3s: e.matmul(
                                dh[:, 512 * q:512 * (q + 1)], dgc,
                                y3s[:, col:col + 512], start=False, stop=True),
                            s_pe if q == 1 else None, 1 if q == 1 else 0)
                sc.wait("scalar", s_pe, sc.val(s_pe))
                sc.wait("scalar", s_v, btot_ready)
                sc.emit("scalar", lambda e, g=g, h=h, gg=gg, dh=dh: e.activation(
                    xslot[:, g % NSLOT, 1024 * h:1024 * (h + 1)], dh,
                    AF.Identity, bias=S_["btot"][:, gg:gg + 1],
                    scale=S_["a4"][:, gg:gg + 1]), relax=True)
                dfree[h % 2] = sc.val(s_s)
                evach[(g, h)] = sc.val(s_s)
            pass2_done[m] = pass2_done_g[g] = sc.val(s_pe)
            evacout_done[g] = sc.val(s_s)
            if m + 1 < NMACRO:
                emit_A2_group(m + 1, gg)
                emit_square(m + 1, gg)

    def emit_stores(m):
        for gg in range(MACRO):
            g = m * MACRO + gg
            for hh in range(2):
                sc.wait("sync", s_s, evach[(g, 2 * hh + 1)])
                sc.emit("sync", lambda e, g=g, hh=hh: e.dma_start(
                    out=out_d[128 * g:128 * (g + 1), 2048 * hh:2048 * (hh + 1)],
                    in_=xslot[:, g % NSLOT, 2048 * hh:2048 * (hh + 1)]),
                    s_st[g % NSLOT], 16)

    def emit_loads(m):
        if m >= NMACRO:
            return
        for gg in range(MACRO):
            g = m * MACRO + gg
            if g >= NSLOT:
                sc.wait("sync", s_st[g % NSLOT], 32 * ((g - NSLOT) // NSLOT + 1))
            sc.emit("sync", lambda e, g=g: e.dma_start(
                out=xslot[:, g % NSLOT, :],
                in_=x_d[128 * g:128 * (g + 1), :]), s_x[g % NSLOT], 16)

    for gg in range(MACRO):
        emit_A2_group(0, gg)
        emit_square(0, gg)
    emit_chain(0)
    emit_B(0)
    for m in range(NMACRO):
        emit_C1(m)
        emit_C2a(m)
        emit_C2b_A2(m)
        if m + 1 < NMACRO:
            emit_chain(m + 1)
            emit_B(m + 1)
        emit_stores(m)
        emit_loads(m + 2)

    for s in range(NSLOT):
        if sc.val(s_st[s]):
            sc.wait("sync", s_st[s], sc.val(s_st[s]))

    with es:
        blk = es.enter_context(nc.Block())

        @blk.sync
        def _(e):
            for f in sc.ops["sync"]:
                f(e)

        @blk.vector
        def _(e):
            for f in sc.ops["vector"]:
                f(e)

        @blk.scalar
        def _(e):
            for f in sc.ops["scalar"]:
                f(e)

        @blk.tensor
        def _(e):
            for f in sc.ops["tensor"]:
                f(e)

        @blk.gpsimd
        def _(e):
            for f in sc.ops["gpsimd"]:
                f(e)

    return nc


_NC_CACHE = []


def kernel(L0_attn_x, L0_ffn_router_w, L0_ffn_A, L0_ffn_B):
    x = np.ascontiguousarray(np.asarray(L0_attn_x, dtype=np.float32).reshape(TOK, D))
    rw = np.asarray(L0_ffn_router_w, dtype=np.float32)
    A = np.asarray(L0_ffn_A, dtype=np.float32)
    B = np.asarray(L0_ffn_B, dtype=np.float32)
    ar = np.ascontiguousarray(
        np.concatenate([A.reshape(64, D).T, rw.T,
                        np.ones((D, 1), np.float32)], axis=1)
        .reshape(32, 128, 69).transpose(1, 0, 2).reshape(128, 2208))
    br = np.ascontiguousarray(
        B.transpose(0, 2, 1).reshape(64, 2, 2048)
        .transpose(1, 0, 2).reshape(128, 2048))
    idh = np.ascontiguousarray(
        (np.arange(128)[None, :] - np.arange(128)[:, None]).astype(np.int32))

    if not _NC_CACHE:
        _NC_CACHE.append(build_program())
    nc = _NC_CACHE[0]
    in_maps = [
        {"x": np.ascontiguousarray(x[i * TPC:(i + 1) * TPC]), "ar": ar,
         "br": br, "idh": idh}
        for i in range(NCORES)
    ]
    res = run_bass_kernel_spmd(nc, in_maps, core_ids=list(range(NCORES)))
    out = np.concatenate([r["out"] for r in res.results], axis=0)
    return out.reshape(1, TOK, D)
